# version 06 (final)
# baseline (speedup 1.0000x reference)
import math
import os
import numpy as np

import concourse.bass as bass
import concourse.mybir as mybir
from concourse.bass_utils import run_bass_kernel_spmd

F32 = mybir.dt.float32
BF16 = mybir.dt.bfloat16
U8 = mybir.dt.uint8
AF = mybir.ActivationFunctionType
ALU = mybir.AluOpType
AX = mybir.AxisListType.X

H = 4096
DSSM = 8192
NH = 64
KCONV = 4
CONV = DSSM + 256
NKT = 32
NCT_X = 64
NCT_CONV = 66
NCT_Z = 64
TOK = 1024
NQ = 8
NCORE = 8
EPS = 1e-5
SCALING = 0.5


def build_M_T_full(dtype=np.float32):
    n = H
    rows = np.arange(n, dtype=np.float64)
    M = np.zeros((n, n), dtype=np.float64)
    M[:, 0] = 1.0 / math.sqrt(n)
    maxf = (n - 1) // 2
    f = np.arange(1, maxf + 1, dtype=np.float64)
    ang = 2.0 * np.pi * np.outer(rows, f) / n
    M[:, 1:1 + 2 * maxf:2] = np.cos(ang) * math.sqrt(2.0 / n)
    M[:, 2:2 + 2 * maxf:2] = np.sin(ang) * math.sqrt(2.0 / n)
    M[:, n - 1] = ((-1.0) ** rows) / math.sqrt(n)
    return np.ascontiguousarray(
        M.astype(dtype).T.reshape(NKT, 128, 8, 512).transpose(2, 0, 1, 3))


class Op:
    __slots__ = ("eng", "sem", "fn", "deps", "inc_amt", "need_inc", "idx",
                 "post", "tag")


class PB:

    def __init__(self):
        self.ops = []
        self.eng_ops = {e: [] for e in ("T", "V", "S", "SYNC", "GP")}
        self.last_w = {}
        self.readers = {}

    def _add(self, eng, sem, fn, r, w, inc_amt, tag):
        op = Op()
        op.eng, op.sem, op.fn, op.inc_amt, op.tag = eng, sem, fn, inc_amt, tag
        op.need_inc = inc_amt == 16
        deps = []
        for key in r:
            lw = self.last_w.get(key)
            if lw is not None:
                deps.append(lw)
            self.readers.setdefault(key, []).append(op)
        for key in w:
            lw = self.last_w.get(key)
            if lw is not None:
                deps.append(lw)
            deps.extend(self.readers.get(key, ()))
            self.last_w[key] = op
            self.readers[key] = []
        seen = set()
        op.deps = []
        for d in deps:
            if d is not op and id(d) not in seen:
                seen.add(id(d))
                op.deps.append(d)
        op.idx = len(self.eng_ops[eng])
        self.eng_ops[eng].append(op)
        self.ops.append(op)
        return op

    def t(self, fn, r=(), w=(), tag=""):
        return self._add("T", "T", fn, r, w, 1, tag)

    def v(self, fn, r=(), w=(), tag=""):
        return self._add("V", "V", fn, r, w, 1, tag)

    def s(self, fn, r=(), w=(), tag=""):
        return self._add("S", "S", fn, r, w, 1, tag)

    def ds(self, fn, r=(), w=(), tag="", sem="sMisc"):
        return self._add("SYNC", sem, fn, r, w, 16, tag)

    def dg(self, fn, r=(), w=(), tag="", sem="gMisc"):
        return self._add("GP", sem, fn, r, w, 16, tag)

    def col(self, fn, r=(), w=(), tag="", sem="cA"):
        return self._add("GP", sem, fn, r, w, 16, tag)

    def finalize(self):
        for op in self.ops:
            for d in op.deps:
                if d.eng != op.eng or d.sem != op.sem:
                    d.need_inc = True
                elif d.sem not in ("T", "V", "S"):
                    d.need_inc = True
        counts = {}
        for op in self.ops:
            if op.need_inc:
                counts[op.sem] = counts.get(op.sem, 0) + op.inc_amt
            op.post = counts.get(op.sem, 0)
        self.final_counts = counts

    def emit(self, eng_name, eng, sems):
        waited = {}
        drain_mark = -1
        for op in self.eng_ops[eng_name]:
            need_drain = False
            for d in op.deps:
                if d.eng == eng_name and d.sem == op.sem and \
                        d.sem in ("T", "V", "S"):
                    if eng_name != "T" and d.idx > drain_mark:
                        need_drain = True
                else:
                    val = d.post
                    if d.sem == "sConst":
                        val = self.final_counts["sConst"]
                    if val > waited.get(d.sem, 0):
                        eng.wait_ge(sems[d.sem], val)
                        waited[d.sem] = val
            if need_drain:
                eng.drain()
                drain_mark = op.idx - 1
            inst = op.fn(eng)
            if op.need_inc:
                assert inst is not None, f"{op.tag}: inc op returned None"
                inst.then_inc(sems[op.sem], op.inc_amt)


def build_program():
    stop_at = os.environ.get("KSTOP", "")
    nc = bass.Bass("TRN2", target_bir_lowering=False)

    def din(name, shape, dt=F32):
        return nc.dram_tensor(name, list(shape), dt, kind="ExternalInput").ap()

    xs = din("xs", (3 + 4096, H))
    wconv = din("wconv", (NCT_CONV, 128, NKT, 128))
    wz = din("wz", (NCT_Z, 128, NKT, 128))
    wdt = din("wdt", (128, NKT, 64))
    wout = din("wout", (8, NCT_X, 128, 512))
    wbt = din("wbt", (8, NKT, 128, 512))
    mt = din("mt", (8, NKT, 128, 512))
    convw_h = din("convw", (128, NCT_CONV * KCONV))
    dtb_h = din("dtb", (64, 1))
    alog_h = din("alog", (64, 1))
    dprow_h = din("dprow", (1, 64))
    nwt_h = din("nwt", (128, 64))
    bb_h = din("bb", (1, H))
    enct_h = din("enct", (NKT, 128, 40))
    coefm_h = din("coefm", (3, NKT, 128, 40))
    maskm_h = din("maskm", (3, NKT, 128, 40), U8)
    identb_h = din("identb", (128, 128), BF16)
    identf_h = din("identf", (128, 128))
    tri_h = din("tri", (128, 128))
    triu_h = din("triu", (128, 128))
    oh127_h = din("oh127", (128, 128))
    id64n_h = din("id64n", (64, 128))
    id64b_h = din("id64b", (64, 64), BF16)

    y_out = nc.dram_tensor("y_out", [TOK, H], F32, kind="ExternalOutput").ap()

    def dint(name, shape, dt):
        return nc.dram_tensor(name, list(shape), dt, kind="Internal").ap()

    xbc_sp = dint("xbc_sp", (NCT_X, 128, TOK), BF16)
    z_sp = dint("z_sp", (NCT_Z, 128, TOK), BF16)
    yg_sp = dint("yg_sp", (NCT_X, 128, TOK), BF16)
    x1_sp = dint("x1_sp", (TOK, H), F32)

    ARENA_BYTES = 212800
    arena = nc.sbuf_tensor("ARENA", [128, ARENA_BYTES // 2], BF16).__enter__()
    PS = nc.psum_tensor([128, 4096], F32).__enter__()
    off = [0]

    def ab(nbytes, align=64):
        o = (off[0] + align - 1) // align * align
        off[0] = o + nbytes
        assert off[0] <= ARENA_BYTES, f"arena overflow {off[0]}"
        return o

    def sb_bf(cols):
        o = ab(cols * 2)
        return arena[:, o // 2:o // 2 + cols]

    def sb_f(cols):
        o = ab(cols * 4)
        return arena[:, o // 2:o // 2 + cols * 2].bitcast(F32)

    xlnT = sb_bf(NKT * 1027)
    Hst = sb_bf(NH * 128)
    B_sb = sb_bf(TOK)
    C_sb = sb_bf(TOK)
    big_o = ab(49152)

    def big_bf(boff, cols):
        return arena[:, (big_o + boff) // 2:(big_o + boff) // 2 + cols]

    def big_f(boff, cols):
        return arena[:, (big_o + boff) // 2:(big_o + boff) // 2 + 2 * cols] \
            .bitcast(F32)

    xbuf = [big_f(0, 4096), big_f(16384, 4096)]
    cme_c = [big_f(32768, 1280), big_f(38912, 1280)]
    xqb = [big_bf(0, 8192), big_bf(16384, 8192)]
    zqb = big_bf(32768, 8192)
    hgb = [big_bf(0, 8192), big_bf(16384, 8192)]
    x1b = [big_f(0, 4096), big_f(16384, 4096)]

    scr_o = ab(8192)
    scr8k = arena[:, scr_o // 2:scr_o // 2 + 4096]
    scr8f = arena[:, scr_o // 2:scr_o // 2 + 4096].bitcast(F32)
    wb_o = ab(8192)
    wb_bf = [arena[:, wb_o // 2:wb_o // 2 + 2048],
             arena[:, wb_o // 2 + 2048:wb_o // 2 + 4096]]
    p1_o = ab(8192)
    p1bf = arena[:, p1_o // 2:p1_o // 2 + 4096]
    w0s_f = p1bf[:, 0:2560].bitcast(F32)
    w0s_b = p1bf[:, 2560:3840]
    stage = [p1bf[:, 0:1024], p1bf[:, 1024:2048]]
    ygbuf = stage
    p2_o = ab(4096)
    p2f = arena[:, p2_o // 2:p2_o // 2 + 2048].bitcast(F32)
    x1scr = [p2f[:, 0:512], p2f[:, 512:1024]]
    x2scr = x1scr
    ca_o = ab(4096)
    caf = arena[:, ca_o // 2:ca_o // 2 + 2048].bitcast(F32)
    convacc = caf[:, 0:1024]
    xresb = [caf[:, 0:512], caf[:, 512:1024]]
    ex_o = ab(2048)
    expd = arena[:, ex_o // 2:ex_o // 2 + 1024].bitcast(F32)[:, 0:512]
    ebf_s = arena[0:40, ex_o // 2:ex_o // 2 + 1024]
    s1s = sb_bf(512)
    cbs = sb_bf(512)
    g1m = sb_f(128)
    bqs = sb_bf(128)
    xt_s = [sb_bf(128), sb_bf(128)]
    xtg_s = [sb_bf(128), sb_bf(128)]
    yr_s = [sb_bf(128), sb_bf(128)]
    y2_s = [sb_bf(128), sb_bf(128)]
    dt_o = ab(4096)
    dtcol = arena[:, dt_o // 2:dt_o // 2 + 2048].bitcast(F32)
    dt_sb = dtcol[0:64, 0:1024]
    cs_o = ab(4096)
    cscol = arena[:, cs_o // 2:cs_o // 2 + 2048].bitcast(F32)
    cs_sb = cscol[0:64, 0:1024]
    bt_o = ab(2048)
    beta_bf = arena[0:64, bt_o // 2:bt_o // 2 + 1024]
    dprow_s = arena[64:65, bt_o // 2:bt_o // 2 + 128].bitcast(F32)
    dp_bf = arena[64:65, bt_o // 2 + 128:bt_o // 2 + 192]

    ldT_sb = sb_f(64)
    dtT_sb = sb_f(64 * NQ)
    dtgT_sb = sb_f(64 * NQ)
    gT_sb = sb_f(64)
    csT_sb = sb_f(64)
    gbc_sb = sb_f(64 * NQ)
    rstd_sb = sb_f(NQ)
    x1sum = sb_f(NQ)
    x1sq = sb_f(NQ)
    tred = sb_f(8)
    mean_s = sb_f(1)
    var_s = sb_f(1)
    sd_s = sb_f(1)
    rst_s = sb_f(1)
    ssum_s = sb_f(1)
    ssq_s = sb_f(1)
    expA = sb_f(1)[0:64, :]
    negA = sb_f(1)[0:64, :]
    dtb_s = sb_f(1)[0:64, :]
    alog_s = sb_f(1)[0:64, :]
    dbc_s = sb_f(64)
    nwt_s = sb_f(64)
    convw_s = sb_f(NCT_CONV * KCONV)
    identb = sb_bf(128)
    identf = sb_f(128)
    tri_s = sb_f(128)
    triu_s = sb_f(128)
    oh127_s = sb_f(128)
    id64n_s = sb_f(128)[0:64, :]
    id64b_s = sb_bf(64)[0:64, :]
    _o1 = sb_bf(128)
    ones1b = _o1[0:1, :]
    ones64b = _o1[64:65, :]
    _o2 = sb_f(128)
    ones1f = _o2[0:1, :]
    ones64f = _o2[64:65, :]
    ones64f8 = _o2[64:72, :]
    o128b = sb_bf(32)[:, 0:1]
    enct_s = sb_bf(NKT * 40)
    vt_o = ab(8192)
    vt_s = arena[0:40, vt_o // 2:vt_o // 2 + 4096]
    bb_s = arena[64:65, vt_o // 2:vt_o // 2 + 4096]

    globals()["DEBUG_APS"] = dict(
        xlnT=xlnT, vt=vt_s, ebf=ebf_s, Hst=Hst, x1sum=x1sum, rstd=rstd_sb,
        x1sq=x1sq, enct=enct_s, bb=bb_s)

    def pbank(b, o=0, n=512):
        return PS[:, 512 * b + o:512 * b + o + n]

    def pbank_bf(b, o=0, n=1024):
        return PS[:, 512 * b:512 * (b + 1)].bitcast(BF16)[:, o:o + n]

    def bank_keys(b):
        return [f"ps{b}"]

    pb = PB()

    pb.v(lambda v: v.memset(ones1b, 1.0), w=["c_1b"], tag="ms1")
    pb.v(lambda v: v.memset(ones64f8, 1.0), w=["c_8f"], tag="ms3")
    pb.v(lambda v: v.memset(ones64b, 1.0), w=["c_1b64"], tag="ms2b")
    pb.v(lambda v: v.memset(o128b, 1.0), w=["c_o128"], tag="ms4")
    pb.v(lambda v: v.memset(Hst[:], 0.0),
         w=[f"H{h}" for h in range(NH)], tag="msH")

    for dst, src, key in ((identb, identb_h, "c_idb"), (identf, identf_h, "c_idf"),
                          (tri_s, tri_h, "c_tri"), (triu_s, triu_h, "c_triu"),
                          (oh127_s, oh127_h, "c_oh"), (id64n_s, id64n_h, "c_idn"),
                          (id64b_s, id64b_h, "c_id64b"), (convw_s, convw_h, "c_cw"),
                          (dtb_s, dtb_h, "c_dtb"), (alog_s, alog_h, "c_alog"),
                          (nwt_s, nwt_h, "c_nwt"), (dprow_s, dprow_h, "c_dp")):
        pb.ds(lambda s, d=dst, so=src: s.dma_start(d, so), w=[key], tag="ldc", sem="sConst")
    pb.dg(lambda g: g.dma_start(enct_s.rearrange("p (k r) -> p k r", r=40),
                                enct_h.rearrange("k p r -> p k r")),
          w=["c_enct"], tag="ld_enct", sem="gEnct")
    pb.dg(lambda g: g.dma_start(bb_s, bb_h), w=["c_bb"], tag="ld_bb", sem="gBb")

    pb.v(lambda v: v.tensor_copy(dp_bf, dprow_s), r=["c_dp"], w=["c_dpb"],
         tag="dpcast")
    pb.t(lambda t: t.matmul(pbank(7, 0, 64), ones64b, dp_bf, start=True,
                            stop=True), r=["c_1b64", "c_dpb"], w=["ps7"],
         tag="mm_dp")
    pb.v(lambda v: v.tensor_copy(dbc_s, pbank(7, 0, 64)), r=["ps7"],
         w=["c_dbc"], tag="ev_dp")

    _n_e = 0 if stop_at == "consts" else 3
    for e in range(_n_e):
        pb.dg(lambda g, e=e: g.dma_start(
            cme_c[0].rearrange("p (k r) -> p k r", r=40),
            coefm_h[e].rearrange("k p r -> p k r")),
            w=["bigC"], tag=f"ldcf{e}", sem=f"gCme{e}")
        pb.dg(lambda g, e=e: g.dma_start(
            cme_c[1].rearrange("p (k r) -> p k r", r=40),
            maskm_h[e].rearrange("k p r -> p k r")),
            w=["bigC2"], tag=f"ldmk{e}", sem=f"gCmeM{e}")
        if e == 0:
            pb.v(lambda v: v.tensor_tensor(w0s_f, cme_c[0], cme_c[1], ALU.mult),
                 r=["bigC", "bigC2"], w=["p1a", "p1b", "p1c"], tag="w0_0")
        else:
            pb.v(lambda v: v.tensor_tensor(scr8f[:, 0:1280], cme_c[0], cme_c[1],
                                           ALU.mult),
                 r=["bigC", "bigC2"], w=["scr"], tag=f"w0m{e}")
            pb.v(lambda v: v.tensor_tensor(w0s_f, w0s_f, scr8f[:, 0:1280],
                                           ALU.add),
                 r=["scr", "p1a", "p1b", "p1c"], w=["p1a", "p1b", "p1c"],
                 tag=f"w0a{e}")
    pb.v(lambda v: v.tensor_copy(w0s_b, w0s_f), r=["p1a", "p1b", "p1c"],
         w=["p1c", "p1d"], tag="w0c")

    for ob in range(0 if stop_at in ("consts", "w0") else 8):
        for kt in range(NKT):
            buf = kt % 2
            pb.dg(lambda g, ob=ob, kt=kt, buf=buf: g.dma_start(
                wb_bf[buf][:, 0:512], mt[ob, kt]),
                w=[f"wb{buf}s0"], tag=f"mtl{ob}_{kt}", sem=f"gWb{buf}")
            pb.t(lambda t, kt=kt, buf=buf: t.matmul(
                pbank(0, 0, 512)[0:40, :], w0s_b[:, kt * 40:kt * 40 + 40],
                wb_bf[buf][:, 0:512],
                start=(kt == 0), stop=(kt == NKT - 1), skip_group_check=True),
                r=["p1c", "p1d", f"wb{buf}s0"],
                w=["ps0"], tag=f"vmm{ob}_{kt}")
        pb.v(lambda v, ob=ob: v.tensor_scalar_mul(
            vt_s[:, ob * 512:ob * 512 + 512], pbank(0, 0, 512)[0:40, :],
            SCALING), r=bank_keys(0), w=["vt"], tag=f"ev_v{ob}")

    def emit_chunk(q, phase2, xq_key, zq_key=None):
        pb.t(lambda t, q=q: t.transpose(
            pbank_bf(2, 768, 128), B_sb[:, q * 128:q * 128 + 128], identb),
            r=["Bsb", "c_idb"], w=["ps2"], tag=f"tpb{q}_{phase2}")
        pb.v(lambda v: v.tensor_copy(bqs, pbank_bf(2, 768, 128)),
             r=["ps2"], w=["bqs"], tag=f"evb{q}_{phase2}")
        if phase2:
            pb.t(lambda t, q=q: t.matmul(
                pbank(2, 0, 128), B_sb[:, q * 128:q * 128 + 128],
                C_sb[:, q * 128:q * 128 + 128], start=True, stop=True),
                r=["Bsb", "Csb"], w=["ps2"], tag=f"mg1{q}")
            pb.v(lambda v: v.tensor_tensor(g1m, pbank(2, 0, 128), tri_s,
                                           ALU.mult),
                 r=["ps2", "c_tri"], w=["g1m"], tag=f"eg1{q}")
        for g in range(16):
            bD = g % 2
            hs = [4 * g + i for i in range(4)]
            if phase2:
                for i, h in enumerate(hs):
                    pb.t(lambda t, h=h, i=i, q=q, bD=bD: t.matmul(
                        pbank(bD, i * 128, 128),
                        id64n_s[:, h:h + 1].broadcast_to([64, 128]),
                        cs_sb[:, q * 128:q * 128 + 128],
                        start=True, stop=False, skip_group_check=True),
                        r=["c_idn", "cs"], w=[f"ps{bD}"], tag=f"d1_{q}_{h}")
                    pb.t(lambda t, h=h, i=i, q=q, bD=bD: t.matmul(
                        pbank(bD, i * 128, 128),
                        cs_sb[:, q * 128:q * 128 + 128],
                        id64n_s[:, 64 + h:64 + h + 1].broadcast_to([64, 128]),
                        start=False, stop=True, skip_group_check=True),
                        r=["c_idn", "cs"], w=[f"ps{bD}"], tag=f"d2_{q}_{h}")
                pb.v(lambda v, bD=bD: v.tensor_scalar_min(
                    pbank(bD, 0, 512), pbank(bD, 0, 512), 0.0),
                    r=bank_keys(bD), w=bank_keys(bD), tag=f"dcl{q}_{g}")
                pb.s(lambda s, bD=bD: s.activation(expd, pbank(bD, 0, 512),
                                                   AF.Exp),
                     r=bank_keys(bD), w=["expd"], tag=f"exd{q}_{g}")
                pb.v(lambda v: v.tensor_tensor(
                    s1s.rearrange("p (r f) -> p r f", r=4),
                    expd.rearrange("p (r f) -> p r f", r=4),
                    g1m.unsqueeze(1).broadcast_to([128, 4, 128]), ALU.mult),
                    r=["expd", "g1m"], w=["s1s"], tag=f"s1_{q}_{g}")
                for i, h in enumerate(hs):
                    pb.t(lambda t, h=h, i=i, q=q, bD=bD: t.matmul(
                        pbank(bD, i * 128, 128),
                        id64b_s[:, h:h + 1].broadcast_to([64, 128]),
                        beta_bf[:, q * 128:q * 128 + 128],
                        start=True, stop=True, skip_group_check=True),
                        r=["c_id64b", "beta"], w=[f"ps{bD}"],
                        tag=f"bb{q}_{h}")
                pb.v(lambda v, q=q, bD=bD: v.tensor_tensor(
                    cbs.rearrange("p (r f) -> p r f", r=4),
                    PS[:, 512 * bD:512 * bD + 512].rearrange(
                        "p (r f) -> p r f", r=4),
                    C_sb[:, q * 128:q * 128 + 128].unsqueeze(1).broadcast_to(
                        [128, 4, 128]), ALU.mult),
                    r=bank_keys(bD) + ["Csb"], w=["cbs"], tag=f"cb{q}_{g}")
            for i, h in enumerate(hs):
                sl = h % 8
                pb.t(lambda t, h=h, q=q: t.transpose(
                    pbank_bf(6 + h % 2, 0, 128),
                    xqb[q % 2][:, h * 128:h * 128 + 128], identb),
                    r=[xq_key, "c_idb"], w=[f"ps{6 + h % 2}"],
                    tag=f"tpx{q}_{h}_{phase2}")
                if phase2:
                    pb.v(lambda v, h=h, q=q, i=i: v.tensor_scalar_mul(
                        xt_s[i % 2], pbank_bf(6 + h % 2, 0, 128),
                        dtT_sb[:, q * 64 + h:q * 64 + h + 1]),
                        r=[f"ps{6 + h % 2}", "dtT"], w=[f"xt{i % 2}"],
                        tag=f"xt{q}_{h}")
                pb.v(lambda v, h=h, q=q, i=i: v.tensor_scalar_mul(
                    xtg_s[i % 2], pbank_bf(6 + h % 2, 0, 128),
                    dtgT_sb[:, q * 64 + h:q * 64 + h + 1]),
                    r=[f"ps{6 + h % 2}", "dtgT"], w=[f"xtg{i % 2}"],
                    tag=f"xg{q}_{h}_{phase2}")
                if phase2:
                    pb.t(lambda t, i=i: t.matmul(
                        pbank(3, i * 128, 128), xt_s[i % 2],
                        s1s[:, i * 128:i * 128 + 128],
                        start=True, stop=False, skip_group_check=True),
                        r=[f"xt{i % 2}", "s1s"], w=[f"ps3q{i}"],
                        tag=f"y1_{q}_{h}")
                    pb.t(lambda t, i=i, h=h: t.matmul(
                        pbank(3, i * 128, 128), Hst[:, h * 128:h * 128 + 128],
                        cbs[:, i * 128:i * 128 + 128],
                        start=False, stop=True, skip_group_check=True),
                        r=[f"H{h}", "cbs"], w=[f"ps3q{i}"], tag=f"y2_{q}_{h}")
                bST = 4 + i % 2
                pb.t(lambda t, i=i, bST=bST: t.matmul(
                    pbank(bST, i * 128, 128), bqs, xtg_s[i % 2],
                    start=True, stop=True, skip_group_check=True),
                    r=["bqs", f"xtg{i % 2}"], w=[f"ps{bST}"],
                    tag=f"st{q}_{h}_{phase2}")
                if phase2:
                    pb.v(lambda v, h=h, i=i, q=q: v.scalar_tensor_tensor(
                        yr_s[i % 2], xqb[q % 2][:, h * 128:h * 128 + 128],
                        dbc_s[:, h:h + 1], pbank(3, i * 128, 128),
                        ALU.mult, ALU.add),
                        r=[xq_key, "c_dbc", f"ps3q{i}"], w=[f"yr{i % 2}"],
                        tag=f"yr{q}_{h}")
                    pb.v(lambda v, h=h, i=i: v.tensor_tensor(
                        zqb[:, h * 128:h * 128 + 128], yr_s[i % 2],
                        zqb[:, h * 128:h * 128 + 128], ALU.mult),
                        r=[f"yr{i % 2}", zq_key], w=[zq_key],
                        tag=f"gt{q}_{h}")
                    pb.s(lambda s, h=h, i=i: s.activation(
                        y2_s[i % 2], zqb[:, h * 128:h * 128 + 128], AF.Square),
                        r=[zq_key], w=[f"y2s{i % 2}"], tag=f"sq{q}_{h}")
                    pb.t(lambda t, h=h, i=i: t.matmul(
                        pbank(2, 192, 1), y2_s[i % 2], o128b,
                        start=(h == 0), stop=(h == NH - 1),
                        skip_group_check=True),
                        r=[f"y2s{i % 2}", "c_o128"], w=["ps2"],
                        tag=f"ss{q}_{h}")
                    pb.v(lambda v, h=h: v.tensor_scalar_mul(
                        zqb[:, h * 128:h * 128 + 128],
                        zqb[:, h * 128:h * 128 + 128], nwt_s[:, h:h + 1]),
                        r=[zq_key, "c_nwt"], w=[zq_key], tag=f"nw{q}_{h}")
                pb.v(lambda v, h=h, i=i, bST=bST, q=q: v.scalar_tensor_tensor(
                    Hst[:, h * 128:h * 128 + 128],
                    Hst[:, h * 128:h * 128 + 128],
                    gbc_sb[:, q * 64 + h:q * 64 + h + 1],
                    pbank(bST, i * 128, 128), ALU.mult, ALU.add),
                    r=[f"H{h}", "gbc", f"ps{bST}"], w=[f"H{h}"],
                    tag=f"su{q}_{h}_{phase2}")
        if phase2:
            pb.v(lambda v: v.tensor_scalar_mul(var_s, pbank(2, 192, 1),
                                               1.0 / DSSM),
                 r=["ps2"], w=["ln5"], tag=f"ssv{q}")
            pb.v(lambda v: v.tensor_scalar_add(var_s, var_s, EPS),
                 r=["ln5"], w=["ln5"], tag=f"sse{q}")
            pb.s(lambda s: s.activation(sd_s, var_s, AF.Sqrt),
                 r=["ln5"], w=["ln4"], tag=f"ssd{q}")
            pb.v(lambda v, q=q: v.reciprocal(rstd_sb[:, q:q + 1], sd_s),
                 r=["ln4"], w=["rstd"], tag=f"ssr{q}")
            pb.ds(lambda s, q=q: s.dma_start(
                yg_sp.rearrange("c p t -> p c t")[:, :, q * 128:q * 128 + 128],
                zqb.rearrange("p (c t) -> p c t", t=128)),
                r=[zq_key], w=[f"d_yg{q}"], tag=f"ygw{q}", sem="sYgW")

    def _truncate():
        pb.v(lambda v: v.memset(x2scr[0], 0.0), w=["p2a"], tag="toms")
        ow = pb.ds(lambda s: s.dma_start(y_out[0:128, 0:512], x2scr[0]),
                   r=["p2a"], w=["d_outX"], tag="towr", sem="sOut0")
        return ow

    if stop_at in ("stage0", "consts", "w0"):
        _truncate()
    _npass = 4
    if stop_at.startswith("pass"):
        _npass = int(stop_at[4]) + 1
    elif stop_at.startswith("p0"):
        _npass = 1
    elif stop_at in ("stage0", "consts", "w0"):
        _npass = 0
    for pp in range(_npass):
        own = pp == 3
        roff = pp * 1024
        for qq in range(NQ + 1):
            if qq == 0:
                rows, rowoff, colbase = 3, roff, 0
            else:
                rows = 128
                rowoff = roff + 3 + (qq - 1) * 128
                colbase = 3 + (qq - 1) * 128
            bk = "bigA" if qq % 2 == 0 else "bigB"
            buf = xbuf[qq % 2]
            pb.ds(lambda s, ro=rowoff, r=rows, b=buf: s.dma_start(
                b[0:r, :], xs[ro:ro + r, :]), w=[bk], tag=f"ldx{pp}_{qq}",
                sem="sBigA" if qq % 2 == 0 else "sBigB")
            pb.v(lambda v, r=rows, b=buf: v.reduce_sum(ssum_s[0:r, :],
                                                       b[0:r, :], AX),
                 r=[bk], w=["ln1"], tag=f"lrs{pp}_{qq}")
            pb.s(lambda s, r=rows, b=buf: s.activation(
                scr8k[0:r, :], b[0:r, :], AF.Square, accum_out=ssq_s[0:r, :]),
                r=[bk], w=["scr", "ln2"], tag=f"lsq{pp}_{qq}")
            pb.v(lambda v, r=rows: v.tensor_scalar_mul(
                mean_s[0:r, :], ssum_s[0:r, :], 1.0 / H),
                r=["ln1"], w=["ln3"], tag=f"lm{pp}_{qq}")
            pb.v(lambda v, r=rows: v.tensor_tensor(
                sd_s[0:r, :], mean_s[0:r, :], mean_s[0:r, :], ALU.mult),
                r=["ln3"], w=["ln4"], tag=f"lt1{pp}_{qq}")
            pb.v(lambda v, r=rows: v.scalar_tensor_tensor(
                var_s[0:r, :], ssq_s[0:r, :], 1.0 / H, sd_s[0:r, :], ALU.mult,
                ALU.subtract), r=["ln2", "ln4"], w=["ln5"], tag=f"lvv{pp}_{qq}")
            pb.v(lambda v, r=rows: v.tensor_scalar_add(
                var_s[0:r, :], var_s[0:r, :], EPS),
                r=["ln5"], w=["ln5"], tag=f"leps{pp}_{qq}")
            pb.s(lambda s, r=rows: s.activation(sd_s[0:r, :], var_s[0:r, :],
                                                AF.Sqrt),
                 r=["ln5"], w=["ln4"], tag=f"lsd{pp}_{qq}")
            pb.v(lambda v, r=rows: v.reciprocal(rst_s[0:r, :], sd_s[0:r, :]),
                 r=["ln4"], w=["ln6"], tag=f"lrc{pp}_{qq}")
            pb.v(lambda v, r=rows, b=buf: v.tensor_scalar(
                scr8k[0:r, :], b[0:r, :], mean_s[0:r, 0:1], rst_s[0:r, 0:1],
                ALU.subtract, ALU.mult), r=[bk, "ln3", "ln6"], w=["scr"],
                tag=f"lxl{pp}_{qq}")
            for kt in range(NKT):
                sl = kt % 8
                pb.t(lambda t, kt=kt, r=rows: t.transpose(
                    pbank_bf(6 + kt % 2, 0, r),
                    scr8k[0:r, kt * 128:kt * 128 + 128],
                    identb[0:r, 0:r]), r=["scr", "c_idb"], w=[f"ps{6 + kt % 2}"],
                    tag=f"tp{pp}_{qq}_{kt}")
                em = pb.v if kt % 2 == 0 else pb.s
                em(lambda e, kt=kt, r=rows, cb=colbase:
                   (e.tensor_copy if kt % 2 == 0 else e.copy)(
                       xlnT[:, kt * 1027 + cb:kt * 1027 + cb + r],
                       pbank_bf(6 + kt % 2, 0, r)),
                   r=[f"ps{6 + kt % 2}"], w=[f"xlnT{kt}"], tag=f"te{pp}_{qq}_{kt}")

        if stop_at == "p0a1":
            break
        pb.dg(lambda g: g.dma_start(
            wb_bf[0].rearrange("p (k c) -> p k c", c=64), wdt),
            w=["wb0s0", "wb0s1"], tag=f"ld_wdt{pp}", sem="gWb0")
        for tb in range(2):
            for kt in range(NKT):
                pb.t(lambda t, tb=tb, kt=kt: t.matmul(
                    pbank(tb, 0, 512)[0:64, :],
                    wb_bf[0][:, kt * 64:kt * 64 + 64],
                    xlnT[:, kt * 1027 + 3 + tb * 512:
                         kt * 1027 + 3 + tb * 512 + 512],
                    start=(kt == 0), stop=(kt == NKT - 1),
                    skip_group_check=True),
                    r=["wb0s0", "wb0s1", f"xlnT{kt}"], w=bank_keys(tb),
                    tag=f"dtmm{pp}_{tb}_{kt}")
        for tb in range(2):
            pb.s(lambda s, tb=tb: s.activation(
                scr8f[0:64, tb * 512:tb * 512 + 512],
                pbank(tb, 0, 512)[0:64, :], AF.Exp, bias=dtb_s[0:64, 0:1]),
                r=bank_keys(tb) + ["c_dtb"], w=["scr"], tag=f"sp1{pp}_{tb}")
            pb.s(lambda s, tb=tb: s.activation(
                dt_sb[:, tb * 512:tb * 512 + 512],
                scr8f[0:64, tb * 512:tb * 512 + 512], AF.Ln, bias=1.0),
                r=["scr"], w=["dt"], tag=f"sp2{pp}_{tb}")
        if pp == 0:
            pb.s(lambda s: s.activation(expA, alog_s, AF.Exp), r=["c_alog"],
                 w=["expA"], tag="exa")
            pb.s(lambda s: s.mul(negA, expA, -1.0), r=["expA"], w=["negA"],
                 tag="nga")
        pb.v(lambda v: v.tensor_scalar_mul(scr8f[0:64, 0:1024], dt_sb,
                                           negA[0:64, 0:1]),
             r=["dt", "negA"], w=["scr"], tag=f"ldv{pp}")
        if stop_at == "p0a2a":
            break
        _lim = 10 ** 9
        if stop_at.startswith("p0a2b") and len(stop_at) > 5:
            _lim = int(stop_at[5:])
        _ctr = [0]

        def _ok():
            _ctr[0] += 1
            return _ctr[0] <= _lim
        ld_f = scr8f[0:64, 0:1024]
        for q in range(1 if stop_at.startswith("p0a2b") else NQ):
            if _ok():
                pb.t(lambda t, q=q: t.transpose(
                    pbank(2, 0, 64), ld_f[:, q * 128:q * 128 + 128],
                    identf[0:64, 0:64]), r=["scr", "c_idf"], w=["ps2"],
                    tag=f"tpl{pp}_{q}")
            if _ok():
                pb.v(lambda v: v.tensor_copy(ldT_sb, pbank(2, 0, 64)),
                     r=["ps2"], w=["ldT"], tag=f"evl{pp}_{q}")
            if _ok():
                pb.t(lambda t: t.matmul(pbank(3, 0, 128)[0:64, :], ldT_sb, tri_s,
                                        start=True, stop=True),
                     r=["ldT", "c_tri"], w=["ps3"], tag=f"mcs{pp}_{q}")
            if _ok():
                pb.t(lambda t: t.matmul(pbank(3, 128, 64), tri_s, ldT_sb,
                                        start=True, stop=True),
                     r=["ldT", "c_tri"], w=["ps3"], tag=f"mct{pp}_{q}")
            if _ok():
                pb.t(lambda t: t.matmul(pbank(3, 192, 64), triu_s, ldT_sb,
                                        start=True, stop=True),
                     r=["ldT", "c_triu"], w=["ps3"], tag=f"mrv{pp}_{q}")
            if _ok():
                pb.v(lambda v, q=q: v.tensor_copy(cs_sb[:, q * 128:q * 128 + 128],
                                                  pbank(3, 0, 128)[0:64, :]),
                     r=["ps3"], w=["cs"], tag=f"ecs{pp}_{q}")
            if _ok():
                pb.v(lambda v: v.tensor_copy(csT_sb, pbank(3, 128, 64)),
                     r=["ps3"], w=["csT"], tag=f"ect{pp}_{q}")
            if _ok():
                pb.v(lambda v: v.tensor_copy(gT_sb, pbank(3, 192, 64)),
                     r=["ps3"], w=["gT"], tag=f"cgt{pp}_{q}")
                pb.s(lambda s: s.activation(gT_sb, gT_sb, AF.Exp),
                     r=["gT"], w=["gT"], tag=f"egt{pp}_{q}")
            if own and _ok():
                pb.s(lambda s, q=q: s.activation(
                    beta_bf[:, q * 128:q * 128 + 128],
                    cs_sb[:, q * 128:q * 128 + 128], AF.Exp),
                    r=["cs"], w=["beta"], tag=f"ebt{q}")
            if _ok():
                pb.t(lambda t: t.matmul(pbank(2, 128, 64), oh127_s, csT_sb,
                                        start=True, stop=True),
                     r=["csT", "c_oh"], w=["ps2"], tag=f"mgb{pp}_{q}")
            if _ok():
                pb.v(lambda v, q=q: v.tensor_copy(
                    gbc_sb[:, q * 64:q * 64 + 64], pbank(2, 128, 64)),
                    r=["ps2"], w=["gbc"], tag=f"cgb{pp}_{q}")
                pb.s(lambda s, q=q: s.activation(
                    gbc_sb[:, q * 64:q * 64 + 64],
                    gbc_sb[:, q * 64:q * 64 + 64], AF.Exp),
                    r=["gbc"], w=["gbc"], tag=f"egb{pp}_{q}")
            if _ok():
                pb.t(lambda t, q=q: t.transpose(
                    pbank(2, 256, 64), dt_sb[:, q * 128:q * 128 + 128],
                    identf[0:64, 0:64]), r=["dt", "c_idf"], w=["ps2"],
                    tag=f"tpd{pp}_{q}")
            if _ok():
                pb.v(lambda v, q=q: v.tensor_copy(dtT_sb[:, q * 64:q * 64 + 64],
                                                  pbank(2, 256, 64)),
                     r=["ps2"], w=["dtT"], tag=f"edt{pp}_{q}")
            if _ok():
                pb.v(lambda v, q=q: v.tensor_tensor(
                    dtgT_sb[:, q * 64:q * 64 + 64], dtT_sb[:, q * 64:q * 64 + 64],
                    gT_sb, ALU.mult), r=["dtT", "gT"], w=["dtgT"],
                    tag=f"dgt{pp}_{q}")

        if stop_at.startswith("p0a2"):
            break
        ncts = NCT_CONV if own else NCT_CONV - 1
        for ct in range(ncts):
            pbase = 4 * (ct % 2)
            for half in range(2):
                buf = (ct + half) % 2
                pb.dg(lambda g, ct=ct, h2=half, buf=buf: g.dma_start(
                    wb_bf[buf].rearrange("p (k c) -> p k c", c=128),
                    wconv[ct, :, h2 * 16:(h2 + 1) * 16, :]),
                    w=[f"wb{buf}s0", f"wb{buf}s1"], tag=f"wc{pp}_{ct}_{half}",
                    sem=f"gWb{buf}")
            for kt in range(NKT):
                buf = (ct + (kt >= 16)) % 2
                for tb in range(4):
                    pb.t(lambda t, ct=ct, kt=kt, tb=tb, pbase=pbase, buf=buf:
                         t.matmul(
                             pbank(pbase + tb, 0, 259),
                             wb_bf[buf][:, (kt % 16) * 128:(kt % 16) * 128 + 128],
                             xlnT[:, kt * 1027 + tb * 256:
                                  kt * 1027 + tb * 256 + 259],
                             start=(kt == 0), stop=(kt == NKT - 1),
                             skip_group_check=True),
                         r=[f"wb{buf}s0", f"wb{buf}s1", f"xlnT{kt}"],
                         w=bank_keys(pbase + tb), tag=f"cmm{pp}_{ct}_{kt}_{tb}")
            ps3d = PS.rearrange("p (b f) -> p b f", f=512)
            acc3 = convacc.rearrange("p (b f) -> p b f", f=256)
            pskeys = sum((bank_keys(pbase + i) for i in range(4)), [])
            for k in range(4):
                if k == 0:
                    pb.v(lambda v, k=k, pbase=pbase, ct=ct: v.tensor_scalar_mul(
                        acc3, ps3d[:, pbase:pbase + 4, k:k + 256],
                        convw_s[:, ct * 4 + k:ct * 4 + k + 1]),
                        r=pskeys + ["c_cw"], w=["cacA", "cacB"],
                        tag=f"cv{pp}_{ct}_{k}")
                else:
                    pb.v(lambda v, k=k, pbase=pbase, ct=ct:
                         v.scalar_tensor_tensor(
                             acc3, ps3d[:, pbase:pbase + 4, k:k + 256],
                             convw_s[:, ct * 4 + k:ct * 4 + k + 1], acc3,
                             ALU.mult, ALU.add),
                         r=pskeys + ["c_cw", "cacA", "cacB"],
                         w=["cacA", "cacB"], tag=f"cv{pp}_{ct}_{k}")
            if ct < NCT_X:
                st = ct % 2
                pb.s(lambda s, st=st: s.activation(stage[st], convacc, AF.Silu),
                     r=["cacA", "cacB"], w=[f"p1{chr(97 + st)}"],
                     tag=f"sil{pp}_{ct}")
                pb.ds(lambda s, ct=ct, st=st: s.dma_start(xbc_sp[ct], stage[st]),
                      r=[f"p1{chr(97 + st)}"], w=[f"d_xbc{ct}"],
                      tag=f"spw{pp}_{ct}", sem=f"sXbcW{st}")
            else:
                dst, key = (B_sb, "Bsb") if ct == NCT_X else (C_sb, "Csb")
                pb.s(lambda s, d=dst: s.activation(d, convacc, AF.Silu),
                     r=["cacA", "cacB"], w=[key], tag=f"sil{pp}_{ct}")

        if stop_at == "p0a3":
            break
        if stop_at == "p0xq":
            for q in range(NQ):
                bk = "bigA" if q % 2 == 0 else "bigB"
                pb.ds(lambda s, q=q: s.dma_start(
                    xqb[q % 2].rearrange("p (c t) -> p c t", t=128),
                    xbc_sp.rearrange("c p t -> p c t")[:, :, q * 128:q * 128 + 128]),
                    r=[f"d_xbc{c}" for c in range(NCT_X)], w=[bk],
                    tag=f"xq{pp}_{q}", sem="sBigA" if q % 2 == 0 else "sBigB")
            break
        if own:
            for ct in range(NCT_Z):
                pb0 = 2 * (ct % 4)
                for half in range(2):
                    buf = (ct + half) % 2
                    pb.dg(lambda g, ct=ct, h2=half, buf=buf: g.dma_start(
                        wb_bf[buf].rearrange("p (k c) -> p k c", c=128),
                        wz[ct, :, h2 * 16:(h2 + 1) * 16, :]),
                        w=[f"wb{buf}s0", f"wb{buf}s1"],
                        tag=f"wzl{ct}_{half}", sem=f"gWb{buf}")
                for kt in range(NKT):
                    buf = (ct + (kt >= 16)) % 2
                    for tb in range(2):
                        pb.t(lambda t, ct=ct, kt=kt, tb=tb, pb0=pb0, buf=buf:
                             t.matmul(
                                 pbank(pb0 + tb, 0, 512),
                                 wb_bf[buf][:, (kt % 16) * 128:
                                            (kt % 16) * 128 + 128],
                                 xlnT[:, kt * 1027 + 3 + tb * 512:
                                      kt * 1027 + 3 + tb * 512 + 512],
                                 start=(kt == 0), stop=(kt == NKT - 1),
                                 skip_group_check=True),
                             r=[f"wb{buf}s0", f"wb{buf}s1", f"xlnT{kt}"],
                             w=bank_keys(pb0 + tb), tag=f"zmm{ct}_{kt}_{tb}")
                for tb in range(2):
                    st = ct % 2
                    pb.s(lambda s, ct=ct, tb=tb, pb0=pb0, st=st: s.activation(
                        stage[st][:, tb * 512:tb * 512 + 512],
                        pbank(pb0 + tb, 0, 512), AF.Silu),
                        r=bank_keys(pb0 + tb), w=[f"p1{chr(97 + st)}"],
                        tag=f"zsil{ct}_{tb}")
                pb.ds(lambda s, ct=ct, st=ct % 2: s.dma_start(z_sp[ct],
                                                              stage[st]),
                      r=[f"p1{chr(97 + st)}"], w=[f"d_z{ct}"],
                      tag=f"zspw{ct}", sem=f"sZW{st}")

        if stop_at == "noscan":
            continue
        for q in range(NQ):
            bk = "bigA" if q % 2 == 0 else "bigB"
            pb.ds(lambda s, q=q: s.dma_start(
                xqb[q % 2].rearrange("p (c t) -> p c t", t=128),
                xbc_sp.rearrange("c p t -> p c t")[:, :, q * 128:q * 128 + 128]),
                r=[f"d_xbc{c}" for c in range(NCT_X)], w=[bk],
                tag=f"xq{pp}_{q}", sem="sBigA" if q % 2 == 0 else "sBigB")
            if own:
                pb.ds(lambda s, q=q: s.dma_start(
                    zqb.rearrange("p (c t) -> p c t", t=128),
                    z_sp.rearrange("c p t -> p c t")[:, :, q * 128:q * 128 + 128]),
                    r=[f"d_z{c}" for c in range(NCT_Z)], w=["bigC", "bigC2"],
                    tag=f"zq_{q}", sem="sBigC")
                emit_chunk(q, True, bk, zq_key="bigC")
            else:
                emit_chunk(q, False, bk)

    if stop_at.startswith("pass") or stop_at.startswith("p0") or \
            stop_at == "noscan":
        _truncate()
    if stop_at == "mixer":
        _truncate()
    def emit_phaseC():
        pb.v(lambda v: v.memset(x1sum, 0.0), w=["x1sum"], tag="msx1")
        pb.v(lambda v: v.memset(x1sq, 0.0), w=["x1sq"], tag="msx2")
        for dblk in range(8):
            for ct in range(NCT_X):
                it = dblk * NCT_X + ct
                yk = f"p1{chr(97 + it % 2)}"
                wk = f"wb{it % 2}s0"
                pb.ds(lambda s, ct=ct, it=it: s.dma_start(ygbuf[it % 2], yg_sp[ct]),
                      r=[f"d_yg{q}" for q in range(NQ)], w=[yk], tag=f"ldy{it}",
                      sem=f"sP1{it % 2}")
                pb.dg(lambda g, dblk=dblk, ct=ct, it=it: g.dma_start(
                    wb_bf[it % 2][:, 0:512], wout[dblk, ct]),
                    w=[wk], tag=f"ldwo{it}", sem=f"gWb{it % 2}")
                for tc in range(8):
                    pb.t(lambda t, tc=tc, ct=ct, it=it: t.matmul(
                        pbank(tc, 0, 512), ygbuf[it % 2][:, tc * 128:tc * 128 + 128],
                        wb_bf[it % 2][:, 0:512],
                        start=(ct == 0), stop=(ct == NCT_X - 1),
                        skip_group_check=True),
                        r=[yk, wk],
                        w=bank_keys(tc),
                        tag=f"omm{it}_{tc}")
            for tc in range(8):
                pb.ds(lambda s, tc=tc, dblk=dblk: s.dma_start(
                    xresb[tc % 2], xs[3075 + tc * 128:3075 + tc * 128 + 128,
                                      dblk * 512:dblk * 512 + 512]),
                    w=[f"cac{chr(65 + tc % 2)}"], tag=f"xr{dblk}_{tc}",
                    sem=f"sCac{tc % 2}")
                pb.v(lambda v, tc=tc: v.scalar_tensor_tensor(
                    x1scr[tc % 2], pbank(tc, 0, 512), rstd_sb[:, tc:tc + 1],
                    xresb[tc % 2], ALU.mult, ALU.add),
                    r=bank_keys(tc) + ["rstd", f"cac{chr(65 + tc % 2)}"],
                    w=[f"p2{chr(97 + tc % 2)}"], tag=f"oev{dblk}_{tc}")
                pb.v(lambda v, tc=tc: v.reduce_sum(tred[:, 0:1], x1scr[tc % 2], AX),
                     r=[f"p2{chr(97 + tc % 2)}"], w=["tr0"], tag=f"or1{dblk}_{tc}")
                pb.v(lambda v, tc=tc: v.tensor_tensor(
                    x1sum[:, tc:tc + 1], x1sum[:, tc:tc + 1], tred[:, 0:1],
                    ALU.add), r=["tr0", "x1sum"], w=["x1sum"], tag=f"oa1{dblk}_{tc}")
                pb.s(lambda s, tc=tc: s.activation(
                    scr8k[:, 0:512], x1scr[tc % 2], AF.Square,
                    accum_out=tred[:, 1:2]),
                    r=[f"p2{chr(97 + tc % 2)}"], w=["scr", "tr1"], tag=f"os1{dblk}_{tc}")
                pb.v(lambda v, tc=tc: v.tensor_tensor(
                    x1sq[:, tc:tc + 1], x1sq[:, tc:tc + 1], tred[:, 1:2], ALU.add),
                    r=["tr1", "x1sq"], w=["x1sq"], tag=f"oa2{dblk}_{tc}")
                pb.ds(lambda s, tc=tc, dblk=dblk: s.dma_start(
                    x1_sp[tc * 128:tc * 128 + 128, dblk * 512:dblk * 512 + 512],
                    x1scr[tc % 2]), r=[f"p2{chr(97 + tc % 2)}"],
                    w=[f"d_x1_{dblk}_{tc}"], tag=f"x1w{dblk}_{tc}", sem=f"sX1W{tc % 2}")
        pb.v(lambda v: v.tensor_scalar_mul(x1sum, x1sum, 1.0 / H),
             r=["x1sum"], w=["x1sum"], tag="mn2")
        pb.v(lambda v: v.tensor_tensor(rstd_sb, x1sum, x1sum, ALU.mult),
             r=["x1sum", "rstd"], w=["rstd"], tag="t2")
        pb.v(lambda v: v.scalar_tensor_tensor(x1sq, x1sq, 1.0 / H, rstd_sb,
                                              ALU.mult, ALU.subtract),
             r=["x1sq", "rstd"], w=["x1sq"], tag="vv2")
        pb.v(lambda v: v.tensor_scalar_add(x1sq, x1sq, EPS),
             r=["x1sq"], w=["x1sq"], tag="eps3")
        pb.s(lambda s: s.activation(x1sq, x1sq, AF.Sqrt),
             r=["x1sq"], w=["x1sq"], tag="sd3")
        pb.v(lambda v: v.reciprocal(rstd_sb, x1sq), r=["x1sq"], w=["rstd"],
             tag="rc3")

    def emit_phaseD():
        for q in range(NQ):
            bk = "bigA" if q % 2 == 0 else "bigB"
            pb.ds(lambda s, q=q: s.dma_start(x1b[q % 2],
                                             x1_sp[q * 128:q * 128 + 128, :]),
                  r=[f"d_x1_{db}_{q}" for db in range(8)], w=[bk],
                  tag=f"ldx1_{q}", sem="sBigA" if q % 2 == 0 else "sBigB")
            pb.v(lambda v, q=q: v.tensor_scalar(
                scr8k, x1b[q % 2], x1sum[:, q:q + 1], rstd_sb[:, q:q + 1],
                ALU.subtract, ALU.mult), r=[bk, "x1sum", "rstd"], w=["scr"],
                tag=f"dxl{q}")
            for kt in range(NKT):
                sl = kt % 8
                pb.t(lambda t, kt=kt: t.transpose(
                    pbank_bf(6 + kt % 2, 0, 128), scr8k[:, kt * 128:kt * 128 + 128],
                    identb), r=["scr", "c_idb"], w=[f"ps{6 + kt % 2}"],
                    tag=f"dtp{q}_{kt}")
                em = pb.v if kt % 2 == 0 else pb.s
                em(lambda e, kt=kt, q=q:
                   (e.tensor_copy if kt % 2 == 0 else e.copy)(
                       xlnT[:, kt * 1027 + 3 + q * 128:
                            kt * 1027 + 3 + q * 128 + 128],
                       pbank_bf(6 + kt % 2, 0, 128)),
                   r=[f"ps{6 + kt % 2}"], w=[f"xlnT{kt}"], tag=f"dte{q}_{kt}")
        for tb in range(2):
            for kt in range(NKT):
                pb.t(lambda t, tb=tb, kt=kt: t.matmul(
                    pbank(tb, 0, 512)[0:40, :], enct_s[:, kt * 40:kt * 40 + 40],
                    xlnT[:, kt * 1027 + 3 + tb * 512:
                         kt * 1027 + 3 + tb * 512 + 512],
                    start=(kt == 0), stop=(kt == NKT - 1), skip_group_check=True),
                    r=["c_enct", f"xlnT{kt}"], w=bank_keys(tb),
                    tag=f"emm{tb}_{kt}")
            pb.v(lambda v, tb=tb: v.tensor_copy(
                ebf_s[:, tb * 512:tb * 512 + 512], pbank(tb, 0, 512)[0:40, :]),
                r=bank_keys(tb), w=["expd"], tag=f"eev{tb}")
        for ob in range(8):
            for kt in range(NKT):
                buf = kt % 2
                pb.dg(lambda g, ob=ob, kt=kt, buf=buf: g.dma_start(
                    wb_bf[buf][:, 0:512], wbt[ob, kt]),
                    w=[f"wb{buf}s0"], tag=f"lwb{ob}_{kt}", sem=f"gWb{buf}")
                for tc in range(8):
                    pb.t(lambda t, kt=kt, tc=tc, buf=buf: t.matmul(
                        pbank(tc, 0, 512),
                        xlnT[:, kt * 1027 + 3 + tc * 128:
                             kt * 1027 + 3 + tc * 128 + 128],
                        wb_bf[buf][:, 0:512],
                        start=(kt == 0), stop=False, skip_group_check=True),
                        r=[f"wb{buf}s0", f"xlnT{kt}"],
                        w=bank_keys(tc),
                        tag=f"fmm{ob}_{kt}_{tc}")
            for tc in range(8):
                pb.t(lambda t, tc=tc, ob=ob: t.matmul(
                    pbank(tc, 0, 512), ebf_s[:, tc * 128:tc * 128 + 128],
                    vt_s[:, ob * 512:ob * 512 + 512],
                    start=False, stop=False, skip_group_check=True),
                    r=["expd", "vt"], w=bank_keys(tc), tag=f"fup{ob}_{tc}")
                pb.t(lambda t, tc=tc, ob=ob: t.matmul(
                    pbank(tc, 0, 512), ones64b, bb_s[:, ob * 512:ob * 512 + 512],
                    start=False, stop=True, skip_group_check=True),
                    r=["c_1b64", "c_bb"], w=bank_keys(tc), tag=f"fbi{ob}_{tc}")
                pb.ds(lambda s, tc=tc, ob=ob: s.dma_start(
                    xresb[tc % 2], x1_sp[tc * 128:tc * 128 + 128,
                                         ob * 512:ob * 512 + 512]),
                    r=[f"d_x1_{ob}_{tc}"], w=[f"cac{chr(65 + tc % 2)}"],
                    tag=f"fx1r{ob}_{tc}", sem=f"sCac{tc % 2}")
                pb.v(lambda v, tc=tc: v.tensor_tensor(
                    x2scr[tc % 2], pbank(tc, 0, 512), xresb[tc % 2], ALU.add),
                    r=bank_keys(tc) + [f"cac{chr(65 + tc % 2)}"], w=[f"p2{chr(97 + tc % 2)}"],
                    tag=f"fev{ob}_{tc}")
                pb.ds(lambda s, tc=tc, ob=ob: s.dma_start(
                    y_out[tc * 128:tc * 128 + 128, ob * 512:ob * 512 + 512],
                    x2scr[tc % 2]), r=[f"p2{chr(97 + tc % 2)}"],
                    w=[f"d_out{ob}_{tc}"], tag=f"fow{ob}_{tc}", sem=f"sOut{tc % 2}")


    if stop_at in ("", "C"):
        emit_phaseC()
    if stop_at == "C":
        _truncate()
    if stop_at == "":
        emit_phaseD()

    pb.finalize()
    sems = {}
    for name in sorted({op.sem for op in pb.ops}):
        sems[name] = nc.semaphore(f"sem_{name}").__enter__()

    blk_cm = nc.Block()
    blk = blk_cm.__enter__()
    blk.tensor(lambda t: pb.emit("T", t, sems))
    blk.vector(lambda v: pb.emit("V", v, sems))
    blk.scalar(lambda s: pb.emit("S", s, sems))
    blk.sync(lambda s: pb.emit("SYNC", s, sems))

    def gp_body(g):
        pb.emit("GP", g, sems)
        for sname in ("sOut0", "sOut1"):
            if sname in pb.final_counts:
                g.wait_ge(sems[sname], pb.final_counts[sname])
    blk.gpsimd(gp_body)
    blk_cm.__exit__(None, None, None)
    return nc


def _prep_inputs(L0_attn_hidden_states, L0_attn_W_in, L0_attn_conv_w,
                 L0_attn_dt_bias, L0_attn_A_log, L0_attn_D_param,
                 L0_attn_norm_weight, L0_attn_W_out, L0_ffn_W_base,
                 L0_ffn_b_base, L0_ffn_enc0, L0_ffn_enc1, L0_ffn_coefs0,
                 L0_ffn_coefs1, L0_ffn_mask0, L0_ffn_mask1):
    import ml_dtypes
    f = np.float32
    x = np.asarray(L0_attn_hidden_states, f)
    W_in = np.asarray(L0_attn_W_in, f)
    conv_w = np.asarray(L0_attn_conv_w, f)
    W_out = np.asarray(L0_attn_W_out, f)
    W_base = np.asarray(L0_ffn_W_base, f)

    wconv = np.ascontiguousarray(
        W_in[:, DSSM:DSSM + CONV].reshape(NKT, 128, NCT_CONV, 128)
        .transpose(2, 1, 0, 3))
    wzr = np.ascontiguousarray(
        W_in[:, :DSSM].reshape(NKT, 128, NCT_Z, 128).transpose(2, 1, 0, 3))
    wdt = np.ascontiguousarray(
        W_in[:, DSSM + CONV:].reshape(NKT, 128, 64).transpose(1, 0, 2))
    woutr = np.ascontiguousarray(
        W_out.reshape(NCT_X, 128, 8, 512).transpose(2, 0, 1, 3))
    wbtr = np.ascontiguousarray(
        W_base.reshape(8, 512, NKT, 128).transpose(0, 2, 3, 1))
    convw_s = np.ascontiguousarray(
        conv_w.reshape(NCT_CONV, 128, KCONV).transpose(1, 0, 2).reshape(128, -1))
    enc = np.concatenate([np.asarray(L0_ffn_enc0, f),
                          np.asarray(L0_ffn_enc1, f)], 0)
    enct = np.ascontiguousarray(enc.T.reshape(NKT, 128, 40))
    coef = np.concatenate([np.asarray(L0_ffn_coefs0, f),
                           np.asarray(L0_ffn_coefs1, f)], 2)
    coefm = np.ascontiguousarray(coef.reshape(3, NKT, 128, 40))
    mask = np.concatenate([np.asarray(L0_ffn_mask0), np.asarray(L0_ffn_mask1)],
                          2)
    maskm = np.ascontiguousarray(mask.reshape(3, NKT, 128, 40)).view(np.uint8)
    nwt = np.ascontiguousarray(
        np.asarray(L0_attn_norm_weight, f).reshape(64, 128).T)

    jj, ii = np.meshgrid(np.arange(128), np.arange(128), indexing="ij")
    common = dict(
        wconv=wconv, wz=wzr, wdt=wdt, wout=woutr, wbt=wbtr, convw=convw_s,
        dtb=np.asarray(L0_attn_dt_bias, f).reshape(64, 1),
        alog=np.asarray(L0_attn_A_log, f).reshape(64, 1),
        dprow=np.asarray(L0_attn_D_param, f).reshape(1, 64),
        nwt=nwt, bb=np.asarray(L0_ffn_b_base, f).reshape(1, H),
        enct=enct, coefm=coefm, maskm=maskm,
        identb=np.eye(128, dtype=ml_dtypes.bfloat16),
        identf=np.eye(128, dtype=f),
        tri=(jj <= ii).astype(f), triu=(jj > ii).astype(f),
        oh127=np.repeat((np.arange(128) == 127).astype(f)[:, None], 128, 1),
        id64n=np.concatenate([np.eye(64, dtype=f), -np.eye(64, dtype=f)], 1),
        id64b=np.eye(64, dtype=ml_dtypes.bfloat16),
    )

    mt_full = build_M_T_full()
    in_maps = []
    for c in range(NCORE):
        b, pos = c // 4, c % 4
        xsh = np.zeros((3 + 4096, H), f)
        base = pos * TOK - 3075
        lo = max(0, -base)
        xsh[lo:] = x[b, base + lo: base + 3 + 4096]
        m = dict(common)
        m.update(xs=xsh, mt=mt_full)
        in_maps.append(m)
    return in_maps


_NC_CACHE = [None]


def kernel(**inputs):
    in_maps = _prep_inputs(**inputs)
    if _NC_CACHE[0] is None:
        _NC_CACHE[0] = build_program()
    nc = _NC_CACHE[0]
    res = run_bass_kernel_spmd(nc, in_maps, core_ids=list(range(NCORE)))
    out = np.empty((2, 4096, H), np.float32)
    for c in range(NCORE):
        b, pos = c // 4, c % 4
        out[b, pos * TOK:(pos + 1) * TOK, :] = res.results[c]["y_out"]
    return out
